# version 11
# baseline (speedup 1.0000x reference)
from contextlib import ExitStack

import numpy as np
import ml_dtypes

import concourse.bass as bass
import concourse.mybir as mybir
from concourse.bass_utils import run_bass_kernel_spmd

F32 = mybir.dt.float32
BF16 = mybir.dt.bfloat16
I32 = mybir.dt.int32
AF = mybir.ActivationFunctionType
ALU = mybir.AluOpType

N_CORES = 8

FULL_CFG = dict(
    T=8192,
    D=2048,
    H=16,
    bounds=[0, 2560, 4096, 5120, 6016, 6656, 7168, 7680, 8192],
)

SMALL_CFG = dict(
    T=2048,
    D=512,
    H=4,
    bounds=[0, 512, 896, 1152, 1408, 1664, 1792, 1920, 2048],
)

ENGS = ("sync", "tensor", "vector", "scalar", "gpsimd")


class Prog:

    def __init__(self, sems):
        self.q = {e: [] for e in ENGS}
        self.c = {name: 0 for name in sems}
        self.sems = sems

    def on(self, eng, fn):
        self.q[eng].append(fn)

    def inc(self, eng, op_fn, sem, n=1):
        h = self.sems[sem]
        self.on(eng, lambda e, f=op_fn, h=h, n=n: f(e).then_inc(h, n))
        self.c[sem] += n
        return self.c[sem]

    def plain(self, eng, op_fn):
        self.on(eng, lambda e, f=op_fn: f(e))

    def vec(self, op_fn, sem="sDve", n=1):
        assert sem == "sDve"
        self.wait("vector", "sDve")
        return self.inc("vector", op_fn, "sDve", n)

    def wait(self, eng, sem, val=None):
        h = self.sems[sem]
        v = self.c[sem] if val is None else val
        assert v <= self.c[sem], (sem, v, self.c[sem])
        self.on(eng, lambda e, h=h, v=v: e.wait_ge(h, v))
        return v


def emit_core(p, c, cfg, t):
    T, D, H = cfg["T"], cfg["D"], cfg["H"]
    DC = D // 128
    a, b = cfg["bounds"][c], cfg["bounds"][c + 1]
    own = b - a
    P = b
    maxown = max(cfg["bounds"][i + 1] - cfg["bounds"][i]
                 for i in range(N_CORES))

    wsb = t["w_sb"]
    NOC = D // 512 if D >= 512 else 1
    A_KH0, A_KH1, A_VH, A_VT = 0, T, 2 * T, 3 * T

    def groups(lo, hi):
        out = []
        g = lo
        while g < hi:
            out.append((g, min(512, hi - g)))
            g += 512
        return out

    def fm_view(dram_ap, col0, w, ncols):
        return dram_ap.rearrange("(dc p) t -> p dc t", p=128)[:, :, col0:col0 + w]

    def stage_kv_view(w):
        return t["stage"][:].rearrange("p (dc t) -> p dc t", t=512)[:, :, 0:w]

    def stage_ot_view(half):
        v = t["stage"][:, half * DC * 128:(half + 1) * DC * 128]
        return v.rearrange("p (dc t) -> p dc t", t=128)

    tct = {"n": 0, "evacs": [("sDve", 0)] * 4}

    def emit_transpose(src_bf16_ap_fn, dst_bf16_ap_fn, eng="vector"):
        k = tct["n"]
        slot = k % 4
        sem = "sDve" if eng == "vector" else "sAct"
        if k >= 2:
            pe, pv = tct["evacs"][(k - 2) % 4]
            p.wait("tensor", pe, pv)
        p.inc("tensor", lambda e, slot=slot, f=src_bf16_ap_fn: e.matmul(
            t["ps_s"][:, slot * 512:slot * 512 + 128],
            f(), t["id_sb"][:], start=True, stop=True), "sTen")
        if eng == "vector":
            p.wait("vector", "sTen")
            v = p.vec(lambda e, slot=slot, g=dst_bf16_ap_fn: e.tensor_copy(
                g(), t["ps_s"][:, slot * 512:slot * 512 + 128]))
        else:
            p.wait("scalar", "sTen")
            v = p.inc("scalar", lambda e, slot=slot, g=dst_bf16_ap_fn:
                      e.copy(g(), t["ps_s"][:, slot * 512:slot * 512 + 128]),
                      "sAct")
        tct["evacs"][slot] = (sem, v)
        tct["n"] = k + 1

    WIDX = {"wk": 0, "wv": 1, "wq": 2, "wo": 3}

    WORD = ("wk", "wv", "wq", "wo")
    SPSEM = {"wv": "sSpW", "wq": "sSpWq", "wo": "sSpWo"}
    bg_q = []
    chunk_no = [0]
    espill_no = [0]

    def mk_chunk(wname, dc):
        base = WIDX[wname] * D

        def emit():
            ping = chunk_no[0] % 2
            chunk_no[0] += 1
            p.wait("scalar", "sDve")
            wv_ = p.inc("scalar", lambda e, dc=dc, w=t[wname], ping=ping:
                        e.dma_start(
                            out=t["wstage"][:, ping * D:(ping + 1) * D],
                            in_=w[dc * 128:(dc + 1) * 128, :]), "sW", 16)
            p.wait("vector", "sW", wv_)
            if wname == "wk":
                p.vec(lambda e, dc=dc, ping=ping: e.tensor_copy(
                    wsb[:, dc * D:(dc + 1) * D],
                    t["wstage"][:, ping * D:(ping + 1) * D]))
                return
            eping = espill_no[0] % 2
            espill_no[0] += 1
            p.wait("vector", "sSpW")
            p.wait("vector", "sSpWq")
            p.wait("vector", "sSpWo")
            cv = p.vec(lambda e, dc=dc, eping=eping, ping=ping: e.tensor_copy(
                t["e_sb"][:, eping * 2048:eping * 2048 + D],
                t["wstage"][:, ping * D:(ping + 1) * D]))
            p.wait("gpsimd", "sDve", cv)
            p.inc("gpsimd", lambda e, dc=dc, base=base, eping=eping:
                  e.dma_start(
                      out=t["wbf_d"].ap()[base + dc * 128:
                                          base + (dc + 1) * 128, :],
                      in_=t["e_sb"][:, eping * 2048:eping * 2048 + D]),
                  SPSEM[wname], 16)
        return emit

    for wname in WORD:
        for dc in range(DC):
            bg_q.append((wname, mk_chunk(wname, dc)))

    def bg_emit(n):
        for _ in range(min(n, len(bg_q))):
            bg_q.pop(0)[1]()

    def bg_flush(wname):
        oi = WORD.index(wname)
        while bg_q and WORD.index(bg_q[0][0]) <= oi:
            bg_q.pop(0)[1]()

    def load_weight_bf16(wname):
        bg_flush(wname)
        base = WIDX[wname] * D
        p.wait("sync", SPSEM[wname])
        p.wait("sync", "sAct")
        p.wait("sync", "sTen")
        p.inc("sync", lambda e, base=base: e.dma_start(
            out=wsb[:, 0:DC * D].rearrange("p (dc n) -> p dc n", n=D),
            in_=t["wbf_d"].ap()[base:base + D, :].rearrange(
                "(dc p) n -> p dc n", p=128)), "sW", 16)

    def emit_ln_group(g0, gw, hb, bgn=2):
        mmv, spv = xnt_state[hb]
        p.wait("vector", "sTen", mmv)
        p.wait("vector", "sSpXnT" if hb == 0 else "sSpXnT1", spv)
        for j in range(gw // 128):
            tok0 = g0 + j * 128
            xoff = (j % 2) * D
            p.wait("sync", "sDve")
            xv = p.inc("sync", lambda e, tok0=tok0, xoff=xoff: e.dma_start(
                out=t["xg"][:, xoff:xoff + D],
                in_=t["x"][tok0:tok0 + 128, :]), "sX", 16)
            p.wait("vector", "sX", xv)
            p.vec(lambda e, xoff=xoff: e.reduce_sum(
                t["nm"][:, 0:1], t["xg"][:, xoff:xoff + D],
                axis=mybir.AxisListType.X, negate=True))
            p.vec(lambda e: e.tensor_scalar_mul(
                t["nm"][:, 0:1], t["nm"][:, 0:1], 1.0 / D))
            p.vec(lambda e, xoff=xoff: e.scalar_tensor_tensor(
                t["sq"][:], t["xg"][:, xoff:xoff + D], 1.0,
                t["xg"][:, xoff:xoff + D], ALU.mult, ALU.mult,
                accum_out=t["ssq"][:, 0:1]))
            p.vec(lambda e: e.tensor_scalar_mul(
                t["var"][:, 0:1], t["ssq"][:, 0:1], 1.0 / D))
            p.vec(lambda e: e.tensor_mul(
                t["m2s"][:, 0:1], t["nm"][:, 0:1], t["nm"][:, 0:1]))
            p.vec(lambda e: e.tensor_sub(
                t["var"][:, 0:1], t["var"][:, 0:1], t["m2s"][:, 0:1]))
            dv = p.vec(lambda e: e.tensor_scalar_add(
                t["var"][:, 0:1], t["var"][:, 0:1], 1e-5), "sDve")
            p.wait("scalar", "sDve", dv)
            sv = p.inc("scalar", lambda e: e.activation(
                t["std"][:, 0:1], t["var"][:, 0:1], AF.Sqrt), "sAct")
            p.wait("vector", "sAct", sv)
            p.vec(lambda e: e.reciprocal(
                t["rstd"][:, 0:1], t["std"][:, 0:1]))
            p.wait("vector", "sTen")
            nv = p.vec(lambda e, xoff=xoff: e.tensor_scalar(
                t["xn"][:], t["xg"][:, xoff:xoff + D], t["nm"][:, 0:1],
                t["rstd"][:, 0:1], ALU.add, ALU.mult), "sDve")
            p.wait("tensor", "sDve", nv)
            for dc in range(DC):
                emit_transpose(
                    lambda dc=dc: t["xn"][:, dc * 128:(dc + 1) * 128],
                    lambda dc=dc, j=j, hb=hb: t["xnT_g"][
                        :, hb * DC * 512 + dc * 512 + j * 128:
                        hb * DC * 512 + dc * 512 + (j + 1) * 128],
                    eng="vector" if dc % 2 == 0 else "scalar")
            bg_emit(bgn)
        p.wait("gpsimd", "sDve")
        p.wait("gpsimd", "sAct")
        p.inc("gpsimd", lambda e, g0=g0, gw=gw, hb=hb: e.dma_start(
            out=fm_view(t["xnT_d"].ap(), g0, gw, T), in_=stage_xnT(gw, hb)),
            "sSpXnT" if hb == 0 else "sSpXnT1", 16)

    def stage_xnT(gw, hb):
        v = t["xnT_g"][:, hb * DC * 512:(hb + 1) * DC * 512]
        return v.rearrange("p (dc t) -> p dc t", t=512)[:, :, 0:gw]

    def proj_pass(wname, dst_dram, glist, colshift, spsem):
        if wname == "wk":
            bg_emit(4)
        else:
            load_weight_bf16(wname)
            p.wait("sync", "sSpXnT")
            p.wait("sync", "sSpXnT1")
        for gi, (g0, gw) in enumerate(glist):
            hb = gi % 2
            base = hb * DC * 512
            if wname == "wk":
                emit_ln_group(g0, gw, hb, bgn=3 if gi == 0 else 2)
                if gi == 0:
                    bg_flush("wk")
            else:
                mmv, spv = xnt_state[hb]
                p.wait("sync", "sTen", mmv)
                rv = p.inc("sync", lambda e, g0=g0, gw=gw, hb=hb: e.dma_start(
                    out=stage_xnT(gw, hb),
                    in_=fm_view(t["xnT_d"].ap(), g0, gw, T)),
                    "sXnTr" if hb == 0 else "sXnTr1", 16)
            p.wait("scalar", "sSpK")
            p.wait("scalar", "sSpV")
            p.wait("scalar", "sSpQ")
            ready_done = [False]
            for m in range(DC):
                acc = t["ps_cd"][:, (m % 2) * 512:(m % 2) * 512 + gw]
                if not ready_done[0]:
                    p.wait("tensor", "sW")
                    if wname == "wk":
                        p.wait("tensor", "sDve")
                    p.wait("tensor", "sAct")
                    if wname != "wk":
                        p.wait("tensor", "sXnTr" if hb == 0 else "sXnTr1")
                    ready_done[0] = True
                if m >= 2:
                    p.wait("tensor", "sAct", evac_val[m % 2])
                for dc in range(DC):
                    is_last = dc == DC - 1
                    fn = lambda e, acc=acc, m=m, dc=dc, gw=gw, base=base: \
                        e.matmul(
                            acc,
                            wsb[:, dc * D + m * 128:dc * D + (m + 1) * 128],
                            t["xnT_g"][:, base + dc * 512:
                                       base + dc * 512 + gw],
                            start=(dc == 0), stop=(dc == DC - 1))
                    if is_last:
                        vt = p.inc("tensor", fn, "sTen")
                    else:
                        p.plain("tensor", fn)
                p.wait("scalar", "sTen", vt)
                evac_val[m % 2] = p.inc(
                    "scalar", lambda e, m=m, gw=gw, acc=acc: e.copy(
                        t["stage"][:, m * 512:m * 512 + gw], acc), "sAct")
            xnt_state[hb] = (p.c["sTen"], p.c["sSpXnT" if hb == 0 else "sSpXnT1"])
            p.wait("gpsimd", "sAct")
            p.inc("gpsimd", lambda e, g0=g0, gw=gw, d=dst_dram, cs=colshift,
                  nc_=(T if dst_dram is not t["q_d"] else maxown):
                  e.dma_start(out=fm_view(d.ap(), g0 - cs, gw, nc_),
                              in_=stage_kv_view(gw)), spsem, 16)
            bg_emit(4)

    evac_val = {0: 0, 1: 0}
    xnt_state = {0: (0, 0), 1: (0, 0)}
    stop_after = cfg.get("stop_after")

    def dbg_dump(src_dram, rows, cols):
        p.wait("sync", "sSpK")
        p.wait("sync", "sSpV")
        p.wait("sync", "sSpQ")
        p.wait("sync", "sSpOT")
        p.inc("sync", lambda e: e.dma_start(
            out=t["qh"][:, 0:cols], in_=src_dram.ap()[0:128, 0:cols]),
            "sKH", 16)
        p.wait("vector", "sKH")
        dv = p.vec(lambda e: e.tensor_copy(
            t["sq"][:, 0:cols], t["qh"][:, 0:cols]))
        p.wait("gpsimd", "sDve", dv)
        p.inc("gpsimd", lambda e: e.dma_start(
            out=t["y"][0:128, 0:cols], in_=t["sq"][:, 0:cols]), "sSpY", 16)
        p.wait("gpsimd", "sSpY")

    if stop_after == "P0":
        return
    pre_groups = groups(0, P)
    own_groups = groups(a, b)
    if stop_after == "W":
        bg_flush("wk")
        p.wait("gpsimd", "sDve")
        return
    if stop_after == "G1":
        bg_flush("wk")
        emit_ln_group(pre_groups[0][0], pre_groups[0][1], 0)
        p.wait("gpsimd", "sSpXnT")
        return
    proj_pass("wk", t["k_d"], pre_groups[:1] if stop_after == "G1K"
              else pre_groups, 0, "sSpK")
    if stop_after == "G1K":
        p.wait("gpsimd", "sSpK")
        return
    if stop_after == "P1a":
        dbg_dump(t["k_d"], 128, min(512, D))
        return
    proj_pass("wv", t["v_d"], pre_groups, 0, "sSpV")
    if stop_after == "P1b":
        p.wait("gpsimd", "sSpV")
        return
    proj_pass("wq", t["q_d"], own_groups, a, "sSpQ")
    if stop_after == "P1":
        dbg_dump(t["k_d"], 128, min(512, D))
        return

    qtiles = groups(a, b)
    NKC = P // 128

    p.wait("sync", "sSpK")
    p.wait("sync", "sSpV")
    p.wait("sync", "sSpQ")
    p.wait("sync", "sSpWq")
    p.wait("sync", "sSpWo")
    p.wait("vector", "sAct")

    def emit_av(q0, qw, nkc, kc, eb, exp_v, mask_v, final=False):
        if kc == 0:
            p.wait("tensor", "sDve")
        p.wait("tensor", "sAct", exp_v)
        if mask_v is not None:
            p.wait("tensor", "sDve", mask_v)
        for half in range(2):
            col = eb * 2048 + half * 512
            p.plain("tensor", lambda e, kc=kc, half=half, col=col, qw=qw,
                    nkc=nkc: e.matmul(
                        t["ps_cd"][:, half * 512:half * 512 + qw],
                        wsb[:, A_VT + kc * 128:A_VT + (kc + 1) * 128],
                        t["e_sb"][:, col:col + qw],
                        start=(kc == 0), stop=(kc == nkc - 1)))
        ret = None
        for half in range(2):
            col = eb * 2048 + half * 512
            row = 32 * half
            fn = lambda e, col=col, row=row, qw=qw, kc=kc, nkc=nkc: e.matmul(
                t["ps_e"][row:row + 1, 0:qw],
                t["ones_col"][:, 0:1],
                t["e_sb"][:, col:col + qw],
                start=(kc == 0), stop=(kc == nkc - 1),
                tile_position=(0, row))
            if final and half == 1:
                ret = p.inc("tensor", fn, "sTen")
            else:
                p.plain("tensor", fn)
        return ret

    khv, qhv, vhv = {}, {}, {}

    def load_kq(h):
        kh0 = A_KH0 if h % 2 == 0 else A_KH1
        ks = "sKH" if h % 2 == 0 else "sKH1"
        qs = "sQH" if h % 2 == 0 else "sQH1"
        khv[h] = (ks, p.inc("sync", lambda e, h=h, kh0=kh0: e.dma_start(
            out=wsb[:, kh0:kh0 + P],
            in_=t["k_d"].ap()[h * 128:(h + 1) * 128, 0:P]), ks, 16))
        qoff0 = (h % 2) * maxown
        qhv[h] = (qs, p.inc("sync", lambda e, h=h, qoff0=qoff0: e.dma_start(
            out=t["qh"][:, qoff0:qoff0 + own],
            in_=t["q_d"].ap()[h * 128:(h + 1) * 128, 0:own]), qs, 16))

    def load_v(h):
        vhv[h] = p.inc("sync", lambda e, h=h: e.dma_start(
            out=wsb[:, A_VH:A_VH + P],
            in_=t["v_d"].ap()[h * 128:(h + 1) * 128, 0:P]), "sVH", 16)

    load_kq(0)
    load_v(0)
    ping_last = {0: 0, 1: 0}

    for h in range(H):
        kh0 = A_KH0 if h % 2 == 0 else A_KH1
        qoff0 = (h % 2) * maxown
        p.wait("tensor", "sDve")
        p.wait("tensor", "sVH", vhv[h])
        for kc in range(NKC):
            emit_transpose(
                lambda kc=kc: wsb[:, A_VH + kc * 128:A_VH + (kc + 1) * 128],
                lambda kc=kc: wsb[:, A_VT + kc * 128:A_VT + (kc + 1) * 128],
                eng="scalar")
        vt_done = p.c["sTen"]
        if h + 1 < H:
            p.wait("sync", "sTen", vt_done)
            load_kq(h + 1)
            load_v(h + 1)
        p.wait("tensor", khv[h][0], khv[h][1])
        p.wait("tensor", qhv[h][0], qhv[h][1])
        p.wait("tensor", "sDve")
        p.wait("tensor", "sAct")

        for (q0, qw) in qtiles:
            qoff = qoff0 + (q0 - a)
            nkc = (q0 + qw) // 128
            from collections import deque
            pend = deque()

            for bi in range(nkc):
                kc = bi
                ping = bi % 2
                eb = bi % 3
                base = ping * 1024
                p.wait("tensor", "sAct", ping_last[ping])
                sc_v = None
                for half in range(2):
                    col = base + half * 512
                    fn = lambda e, kc=kc, half=half, col=col, qw=qw, \
                        qoff=qoff, kh0=kh0: e.matmul(
                            t["ps_s"][:, col:col + qw],
                            wsb[half * 64:(half + 1) * 64,
                                kh0 + kc * 128:kh0 + (kc + 1) * 128],
                            t["qh"][half * 64:(half + 1) * 64,
                                    qoff:qoff + qw],
                            start=True, stop=True,
                            tile_position=(half * 64, 0))
                    if half == 1:
                        sc_v = p.inc("tensor", fn, "sTen")
                    else:
                        p.plain("tensor", fn)
                if len(pend) >= 2:
                    emit_av(q0, qw, nkc, *pend.popleft())
                p.wait("scalar", "sTen", sc_v)
                width = 512 + qw
                ping_last[ping] = p.inc(
                    "scalar", lambda e, eb=eb, base=base, width=width:
                    e.activation(
                        t["e_sb"][:, eb * 2048:eb * 2048 + width],
                        t["ps_s"][:, base:base + width],
                        AF.Exp, scale=t["const"][:, 0:1]), "sAct")
                mask_v = None
                if kc * 128 >= q0:
                    cstart = kc * 128 - q0
                    p.wait("vector", "sAct", ping_last[ping])
                    for half in range(2):
                        col = eb * 2048 + half * 512
                        if cstart > 0:
                            p.vec(lambda e, col=col, cs=cstart: e.memset(
                                t["e_sb"][:, col:col + cs], 0.0))
                        fn = lambda e, col=col, cs=cstart: e.tensor_mul(
                            t["e_sb"][:, col + cs:col + cs + 128],
                            t["e_sb"][:, col + cs:col + cs + 128],
                            t["tri"][:])
                        mask_v = p.vec(fn)
                pend.append((kc, eb, ping_last[ping], mask_v))

            while len(pend) > 1:
                emit_av(q0, qw, nkc, *pend.popleft())
            last_mm = emit_av(q0, qw, nkc, *pend.popleft(), final=True)

            p.wait("vector", "sTen", last_mm)
            p.wait("vector", "sSpOT")
            p.vec(lambda e, qw=qw: e.tensor_copy(
                t["sra"][:, 0:qw], t["ps_e"][0:64, 0:qw]))
            def _recip(e, qw=qw):
                with t["nc"].allow_low_precision(
                        reason="softmax denominators broadcast in bf16"):
                    return e.reciprocal(t["rr"][0:33, 0:qw],
                                        t["sra"][0:33, 0:qw])
            rec_v = p.vec(_recip)
            p.wait("tensor", "sDve", rec_v)
            p.plain("tensor", lambda e, qw=qw: e.matmul(
                t["ps_sp"][:, 0:qw], t["ones_row"][0:1, :],
                t["rr"][0:1, 0:qw], start=True, stop=True))
            bc_v = p.inc("tensor", lambda e, qw=qw: e.matmul(
                t["ps_e"][:, 0:qw], t["mones_row"][32:33, :],
                t["rr"][32:33, 0:qw], start=True, stop=True), "sTen")
            p.wait("vector", "sTen", bc_v)
            p.vec(lambda e, qw=qw: e.tensor_copy(
                t["fb"][:, 0:qw], t["ps_sp"][:, 0:qw]))
            p.vec(lambda e, qw=qw: e.tensor_copy(
                t["fb"][:, 512:512 + qw], t["ps_e"][:, 0:qw]))
            p.vec(lambda e, qw=qw: e.tensor_mul(
                t["m1"][:, 0:qw], t["ps_cd"][:, 0:qw], t["fb"][:, 0:qw]))
            p.vec(lambda e, qw=qw: e.tensor_mul(
                t["m2"][:, 0:qw], t["ps_cd"][:, 512:512 + qw],
                t["fb"][:, 512:512 + qw]))
            p.vec(lambda e, qw=qw, h=h: e.tensor_scalar(
                t["m2"][:, 0:qw], t["m2"][:, 0:qw],
                t["lam_bc"][:, h:h + 1], None, ALU.mult))
            cmb_v = p.vec(lambda e, qw=qw: e.tensor_add(
                t["oT"][:, 0:qw], t["m1"][:, 0:qw], t["m2"][:, 0:qw]))
            p.wait("gpsimd", "sDve", cmb_v)
            p.inc("gpsimd", lambda e, h=h, qoff=qoff, q0=q0, qw=qw:
                  e.dma_start(
                      out=t["oT_d"].ap()[h * 128:(h + 1) * 128,
                                         q0 - a:q0 - a + qw],
                      in_=t["oT"][:, 0:qw]), "sSpOT", 16)

    if stop_after == "P2":
        dbg_dump(t["oT_d"], 128, min(512, own))
        return
    NTT = own // 128
    p.wait("sync", "sSpOT")
    p.wait("vector", "sTen")
    load_weight_bf16("wo")

    for i in range(NTT):
        half = i % 2
        p.wait("sync", "sTen")
        otr = p.inc("sync", lambda e, i=i, half=half: e.dma_start(
            out=stage_ot_view(half),
            in_=fm_view(t["oT_d"].ap(), i * 128, 128, maxown)), "sOTr", 16)
        p.wait("sync", "sDve")
        xv = p.inc("sync", lambda e, i=i, half=half: e.dma_start(
            out=t["xg"][:, half * D:half * D + D],
            in_=t["x"][a + i * 128:a + i * 128 + 128, :]), "sX", 16)
        p.wait("tensor", "sOTr", otr)
        p.wait("tensor", "sW")
        p.wait("tensor", "sDve")
        wo_v = None
        for oc in range(NOC):
            for hc in range(DC):
                is_last = (oc == NOC - 1) and (hc == DC - 1)
                fn = lambda e, oc=oc, hc=hc, half=half: e.matmul(
                    t["ps_s"][:, oc * 512:(oc + 1) * 512],
                    t["stage"][:, half * DC * 128 + hc * 128:
                               half * DC * 128 + (hc + 1) * 128],
                    wsb[:, hc * D + oc * 512:hc * D + (oc + 1) * 512],
                    start=(hc == 0), stop=(hc == DC - 1))
                if is_last:
                    wo_v = p.inc("tensor", fn, "sTen")
                else:
                    p.plain("tensor", fn)
        p.wait("vector", "sTen", wo_v)
        p.wait("vector", "sX", xv)
        p.wait("vector", "sSpX2")
        for oc in range(NOC):
            p.vec(lambda e, oc=oc, half=half: e.tensor_add(
                t["sq"][:, oc * 512:(oc + 1) * 512],
                t["ps_s"][:, oc * 512:(oc + 1) * 512],
                t["xg"][:, half * D + oc * 512:half * D + (oc + 1) * 512]))
        p.vec(lambda e: e.reduce_sum(
            t["nm"][:, 0:1], t["sq"][:], axis=mybir.AxisListType.X,
            negate=True))
        x2_v = p.vec(lambda e, i=i: e.tensor_scalar_mul(
            t["stats"][:, i:i + 1], t["nm"][:, 0:1], 1.0 / D),
            "sDve")
        p.wait("scalar", "sDve", x2_v)
        sq_v = p.inc("scalar", lambda e: e.activation(
            t["wstage"][:, 0:D], t["sq"][:], AF.Square,
            accum_out=t["ssq"][:, 0:1]), "sAct")
        p.wait("vector", "sAct", sq_v)
        p.vec(lambda e: e.tensor_scalar_mul(
            t["var"][:, 0:1], t["ssq"][:, 0:1], 1.0 / D))
        p.vec(lambda e, i=i: e.tensor_mul(
            t["m2s"][:, 0:1], t["stats"][:, i:i + 1],
            t["stats"][:, i:i + 1]))
        p.vec(lambda e: e.tensor_sub(
            t["var"][:, 0:1], t["var"][:, 0:1], t["m2s"][:, 0:1]))
        st_v = p.vec(lambda e, i=i, NTT=NTT: e.tensor_scalar_add(
            t["stats"][:, NTT + i:NTT + i + 1], t["var"][:, 0:1], 1e-5),
            "sDve")
        p.wait("gpsimd", "sDve", st_v)
        p.inc("gpsimd", lambda e, i=i: e.dma_start(
            out=t["x2_d"].ap()[i * 128:(i + 1) * 128, :], in_=t["sq"][:]),
            "sSpX2", 16)

    p.wait("scalar", "sDve")
    rs_v = p.inc("scalar", lambda e, NTT=NTT: e.activation(
        t["stats2"][:, 0:NTT], t["stats"][:, NTT:2 * NTT], AF.Sqrt),
        "sAct")
    p.wait("vector", "sAct", rs_v)
    p.vec(lambda e, NTT=NTT: e.reciprocal(
        t["stats3"][:, 0:NTT], t["stats2"][:, 0:NTT]), "sDve")

    p.wait("vector", "sSpX2")
    for och in range(DC):
        hw = och % 2
        if och < 2:
            p.wait("sync", "sAct")
            p.wait("sync", "sDve")
        else:
            p.wait("sync", "sAct")
            p.wait("sync", "sDve")
            p.wait("sync", "sW")
        p.inc("sync", lambda e, och=och, hw=hw: e.dma_start(
            out=t["wstage"][:, hw * D:(hw + 1) * D],
            in_=t["fw"][och * 128:(och + 1) * 128, :]), "sW", 16)
        wv_ = p.inc("sync", lambda e, och=och, hw=hw: e.dma_start(
            out=t["xg"][:, hw * D:(hw + 1) * D],
            in_=t["frr"][och * 128:(och + 1) * 128, :]), "sW", 16)
        p.wait("scalar", "sW", wv_)
        p.wait("scalar", "sDve")
        p.plain("scalar", lambda e, hw=hw: e.activation(
            t["e_sb"][:, 0:D], t["wstage"][:, hw * D:(hw + 1) * D], AF.Sign,
            bias=t["const"][:, 3:4]))
        sg_v = p.inc("scalar", lambda e, hw=hw: e.activation(
            t["e_sb"][:, 2048:2048 + D], t["wstage"][:, hw * D:(hw + 1) * D],
            AF.Sign, bias=t["const"][:, 4:5]), "sAct")
        p.wait("vector", "sAct", sg_v)
        p.vec(lambda e: e.tensor_add(
            t["xn"][:], t["e_sb"][:, 0:D], t["e_sb"][:, 2048:2048 + D]))
        p.wait("vector", "sTen")
        wr_v = p.vec(lambda e, hw=hw: e.scalar_tensor_tensor(
            t["e_sb"][:, 0:D], t["xn"][:], 0.3,
            t["xg"][:, hw * D:(hw + 1) * D], ALU.mult, ALU.add), "sDve")
        p.wait("tensor", "sDve", wr_v)
        for dc in range(DC):
            emit_transpose(
                lambda dc=dc: t["e_sb"][:, dc * 128:(dc + 1) * 128],
                lambda dc=dc, och=och: wsb[:, dc * D + och * 128:
                                           dc * D + och * 128 + 128],
                eng="vector" if dc % 2 == 0 else "scalar")

    p.wait("vector", "sSpX2")
    p.wait("sync", "sSpX2")
    for i in range(NTT):
        half = i % 2
        p.wait("sync", "sDve")
        x2r = p.inc("sync", lambda e, i=i, half=half: e.dma_start(
            out=t["xg"][:, half * D:half * D + D],
            in_=t["x2_d"].ap()[i * 128:(i + 1) * 128, :]), "sX2r", 16)
        p.wait("vector", "sX2r", x2r)
        p.wait("vector", "sAct")
        nv = p.vec(lambda e, i=i, half=half: e.tensor_scalar(
            t["xn"][:], t["xg"][:, half * D:half * D + D],
            t["stats"][:, i:i + 1],
            t["stats3"][:, i:i + 1],
            ALU.add, ALU.mult), "sDve")
        p.wait("scalar", "sDve", nv)
        p.plain("scalar", lambda e: e.activation(
            t["e_sb"][:, 0:D], t["xn"][:], AF.Sign, bias=t["const"][:, 1:2]))
        sg_v = p.inc("scalar", lambda e: e.activation(
            t["e_sb"][:, 2048:2048 + D], t["xn"][:], AF.Sign,
            bias=t["const"][:, 2:3]), "sAct")
        p.wait("vector", "sAct", sg_v)
        p.vec(lambda e: e.tensor_add(
            t["xn"][:], t["e_sb"][:, 0:D], t["e_sb"][:, 2048:2048 + D]))
        p.wait("vector", "sTen")
        xq_v = p.vec(lambda e: e.tensor_scalar(
            t["e_sb"][:, 0:D], t["xn"][:], 0.25, None, ALU.mult), "sDve")
        p.wait("tensor", "sDve", xq_v)
        for dc in range(DC):
            emit_transpose(
                lambda dc=dc: t["e_sb"][:, dc * 128:(dc + 1) * 128],
                lambda dc=dc, half=half: t["stage"][
                    :, half * DC * 128 + dc * 128:
                    half * DC * 128 + (dc + 1) * 128])
        xqT_v = p.wait("tensor", "sDve")
        ffn_v = None
        for oc in range(NOC):
            for dc in range(DC):
                is_last = (oc == NOC - 1) and (dc == DC - 1)
                fn = lambda e, oc=oc, dc=dc, half=half: e.matmul(
                    t["ps_s"][:, oc * 512:(oc + 1) * 512],
                    t["stage"][:, half * DC * 128 + dc * 128:
                               half * DC * 128 + (dc + 1) * 128],
                    wsb[:, dc * D + oc * 512:dc * D + (oc + 1) * 512],
                    start=(dc == 0), stop=(dc == DC - 1))
                if is_last:
                    ffn_v = p.inc("tensor", fn, "sTen")
                else:
                    p.plain("tensor", fn)
        p.wait("vector", "sTen", ffn_v)
        p.wait("vector", "sSpY")
        out_v = None
        for oc in range(NOC):
            is_last = oc == NOC - 1
            fn = lambda e, oc=oc, half=half: e.tensor_add(
                t["sq"][:, oc * 512:(oc + 1) * 512],
                t["ps_s"][:, oc * 512:(oc + 1) * 512],
                t["xg"][:, half * D + oc * 512:half * D + (oc + 1) * 512])
            if is_last:
                out_v = p.vec(fn, "sDve")
            else:
                p.vec(fn)
        p.wait("gpsimd", "sDve", out_v)
        p.inc("gpsimd", lambda e, i=i: e.dma_start(
            out=t["y"][a + i * 128:a + i * 128 + 128, :], in_=t["sq"][:]),
            "sSpY", 16)

    p.wait("gpsimd", "sSpY")


def build_program(cfg):
    T, D, H = cfg["T"], cfg["D"], cfg["H"]
    DC = D // 128
    maxown = max(cfg["bounds"][i + 1] - cfg["bounds"][i]
                 for i in range(N_CORES))
    w_cols = max(DC * D, 4 * T)

    nc = bass.Bass("TRN2", target_bir_lowering=False)
    t = {"nc": nc}
    t["x"] = nc.dram_tensor("x", [T, D], F32, kind="ExternalInput").ap()
    for n_, nm in [("wq", "wq"), ("wk", "wk"), ("wv", "wv"), ("wo", "wo"),
                   ("fw", "fw"), ("frr", "frr")]:
        t[n_] = nc.dram_tensor(nm, [D, D], F32, kind="ExternalInput").ap()
    t["lam0"] = nc.dram_tensor("lam0", [1, H], F32, kind="ExternalInput").ap()
    t["cid"] = nc.dram_tensor("cid", [1, 1], I32, kind="ExternalInput").ap()
    t["tri_in"] = nc.dram_tensor("tri_in", [128, 128], BF16,
                                 kind="ExternalInput").ap()
    t["id_in"] = nc.dram_tensor("id_in", [128, 128], BF16,
                                kind="ExternalInput").ap()
    t["id32_in"] = nc.dram_tensor("id32_in", [128, 128], F32,
                                  kind="ExternalInput").ap()
    t["y"] = nc.dram_tensor("y", [T, D], F32, kind="ExternalOutput").ap()

    t["xnT_d"] = nc.dram_tensor("xnT_d", [D, T], BF16)
    t["k_d"] = nc.dram_tensor("k_d", [D, T], BF16)
    t["v_d"] = nc.dram_tensor("v_d", [D, T], BF16)
    t["q_d"] = nc.dram_tensor("q_d", [D, maxown], BF16)
    t["oT_d"] = nc.dram_tensor("oT_d", [D, maxown], BF16)
    t["x2_d"] = nc.dram_tensor("x2_d", [maxown, D], F32)
    t["wbf_d"] = nc.dram_tensor("wbf_d", [4 * D, D], BF16)

    sem_names = ["sIn", "sX", "sW", "sKH", "sKH1", "sVH", "sQH", "sQH1", "sOTr", "sX2r",
                 "sXnTr", "sXnTr1", "sSpW", "sSpWq", "sSpWo", "sSpXnT", "sSpXnT1", "sSpK", "sSpV", "sSpQ", "sSpOT",
                 "sSpX2", "sSpY", "sTen", "sAct", "sDve"]

    with ExitStack() as ctx:
        sb = lambda n, s, d: ctx.enter_context(nc.sbuf_tensor(n, s, d))
        psa = lambda n, s, d: ctx.enter_context(nc.psum_tensor(n, s, d))

        t["cid_sb"] = sb("cid_sb", [1, 1], I32)
        t["const"] = sb("const", [128, 8], F32)
        t["tri"] = sb("tri", [128, 128], BF16)
        t["id_sb"] = sb("id_sb", [128, 128], BF16)
        t["id32"] = sb("id32", [128, 128], F32)
        t["ones_col"] = sb("ones_col", [128, 1], BF16)
        t["ones_row"] = sb("ones_row", [1, 128], BF16)
        t["ones_row32"] = sb("ones_row32", [1, 128], F32)
        t["mones_row"] = sb("mones_row", [33, 128], BF16)
        t["lamraw"] = sb("lamraw", [1, H], F32)
        t["lam"] = sb("lam", [1, H], F32)
        t["lamt"] = sb("lamt", [1, H], F32)
        t["lam_bc"] = sb("lam_bc", [128, H], F32)

        t["w_sb"] = sb("w_sb", [128, w_cols], BF16)
        t["wstage"] = sb("wstage", [128, 2 * D], F32)
        t["qh"] = sb("qh", [128, 2 * maxown], BF16)
        t["stage"] = sb("stage", [128, DC * 512], BF16)
        t["e_sb"] = sb("e_sb", [128, 6144], BF16)
        t["xg"] = sb("xg", [128, 2 * D], F32)
        t["sq"] = sb("sq", [128, D], F32)
        t["xn"] = sb("xn", [128, D], BF16)
        t["xnT_g"] = sb("xnT_g", [128, 2 * DC * 512], BF16)
        t["fb"] = sb("fb", [128, 1024], F32)
        t["m1"] = sb("m1", [128, 512], F32)
        t["m2"] = sb("m2", [128, 512], F32)
        t["oT"] = sb("oT", [128, 512], BF16)
        t["sra"] = sb("sra", [64, 512], F32)
        t["srb"] = sb("srb", [64, 512], F32)
        t["rr"] = sb("rr", [64, 512], BF16)
        t["nm"] = sb("nm", [128, 1], F32)
        t["ssq"] = sb("ssq", [128, 1], F32)
        t["var"] = sb("var", [128, 1], F32)
        t["std"] = sb("std", [128, 1], F32)
        t["rstd"] = sb("rstd", [128, 1], F32)
        t["m2s"] = sb("m2s", [128, 1], F32)
        t["stats"] = sb("stats", [128, 2 * (maxown // 128)], F32)
        t["stats2"] = sb("stats2", [128, 2 * (maxown // 128)], F32)
        t["stats3"] = sb("stats3", [128, 2 * (maxown // 128)], F32)

        t["ps_s"] = psa("ps_s", [128, 2048], F32)
        t["ps_cd"] = psa("ps_cd", [128, 1024], F32)
        t["ps_e"] = psa("ps_e", [128, 512], F32)
        t["ps_sp"] = psa("ps_sp", [128, 512], F32)

        sems = {n: ctx.enter_context(nc.semaphore(n)) for n in sem_names}
        regs = {e: ctx.enter_context(getattr(nc, e).register("r_" + e))
                for e in ENGS}
        block = ctx.enter_context(nc.Block())

        progs = []
        for c in range(N_CORES):
            p = Prog(sems)
            p.c["sIn"] = 80
            p.c["sDve"] = 15
            p.c["sTen"] = 1
            p.c["sAct"] = 1
            emit_core(p, c, cfg, t)
            progs.append(p)

        def replay(eng_name, e):
            e.wait_ge(sems["sIn"], 80)
            e.reg_load(regs[eng_name], t["cid_sb"][0:1, 0:1])
            if eng_name == "vector":
                e.memset(t["const"][:, 0:1], 0.125).then_inc(sems["sDve"], 1)
                e.memset(t["const"][:, 1:2], -0.33).then_inc(sems["sDve"], 1)
                e.memset(t["const"][:, 2:3], 0.33).then_inc(sems["sDve"], 1)
                e.memset(t["const"][:, 3:4], -0.2).then_inc(sems["sDve"], 1)
                e.memset(t["const"][:, 4:5], 0.2).then_inc(sems["sDve"], 1)
                e.memset(t["ones_col"][:], 1.0).then_inc(sems["sDve"], 1)
                e.memset(t["ones_row"][:], 1.0).then_inc(sems["sDve"], 1)
                e.memset(t["mones_row"][:], -1.0).then_inc(sems["sDve"], 1)
                e.memset(t["ones_row32"][:], 1.0).then_inc(sems["sDve"], 1)
                e.memset(t["ps_s"][:], 0.0).then_inc(sems["sDve"], 1)
                e.memset(t["ps_cd"][:], 0.0).then_inc(sems["sDve"], 1)
                e.memset(t["ps_e"][:], 0.0).then_inc(sems["sDve"], 1)
            if eng_name == "scalar":
                e.wait_ge(sems["sDve"], 1)
                e.activation(t["lamt"][0:1, :], t["lamraw"][0:1, :], AF.Exp,
                             scale=-1.0).then_inc(sems["sAct"], 1)
            if eng_name == "vector":
                e.wait_ge(sems["sAct"], 1)
                e.wait_ge(sems["sDve"], 12)
                e.tensor_scalar_add(t["lamt"][0:1, :], t["lamt"][0:1, :],
                                    1.0).then_inc(sems["sDve"], 1)
                e.wait_ge(sems["sDve"], 13)
                e.reciprocal(t["lam"][0:1, :], t["lamt"][0:1, :]).then_inc(
                    sems["sDve"], 1)
            if eng_name == "tensor":
                e.wait_ge(sems["sDve"], 14)
                e.matmul(t["ps_e"][:, 0:H], t["ones_row32"][0:1, :],
                         t["lam"][0:1, :], start=True, stop=True).then_inc(
                    sems["sTen"], 1)
            if eng_name == "vector":
                e.wait_ge(sems["sTen"], 1)
                e.wait_ge(sems["sDve"], 14)
                e.tensor_copy(t["lam_bc"][:, 0:H],
                              t["ps_e"][:, 0:H]).then_inc(sems["sDve"], 1)
            IF_CHUNK = 1 << 30
            for c in range(N_CORES):
                q = progs[c].q[eng_name]
                for i0 in range(0, len(q), IF_CHUNK):
                    with e.If_eq(regs[eng_name], c):
                        for fn in q[i0:i0 + IF_CHUNK]:
                            fn(e)

        @block.sync
        def _(sync):
            sync.dma_start(out=t["cid_sb"][:], in_=t["cid"]).then_inc(
                sems["sIn"], 16)
            sync.dma_start(out=t["tri"][:], in_=t["tri_in"]).then_inc(
                sems["sIn"], 16)
            sync.dma_start(out=t["id_sb"][:], in_=t["id_in"]).then_inc(
                sems["sIn"], 16)
            sync.dma_start(out=t["id32"][:], in_=t["id32_in"]).then_inc(
                sems["sIn"], 16)
            sync.dma_start(out=t["lamraw"][0:1, :], in_=t["lam0"]).then_inc(
                sems["sIn"], 16)
            replay("sync", sync)

        @block.tensor
        def _(e):
            replay("tensor", e)

        @block.vector
        def _(e):
            replay("vector", e)

        @block.scalar
        def _(e):
            replay("scalar", e)

        @block.gpsimd
        def _(e):
            replay("gpsimd", e)

    return nc


def make_consts():
    tri = (np.arange(128)[:, None] <= np.arange(128)[None, :]).astype(
        ml_dtypes.bfloat16)
    ident = np.eye(128, dtype=ml_dtypes.bfloat16)
    return tri, ident


def run_cfg(cfg, x, wq, wk, wv, wo, lam0, fw, frr):
    T, D, H = cfg["T"], cfg["D"], cfg["H"]
    tri, ident = make_consts()
    base = {
        "x": np.ascontiguousarray(x, dtype=np.float32),
        "wq": np.ascontiguousarray(wq, dtype=np.float32),
        "wk": np.ascontiguousarray(wk, dtype=np.float32),
        "wv": np.ascontiguousarray(wv, dtype=np.float32),
        "wo": np.ascontiguousarray(wo, dtype=np.float32),
        "fw": np.ascontiguousarray(fw, dtype=np.float32),
        "frr": np.ascontiguousarray(frr, dtype=np.float32),
        "lam0": np.ascontiguousarray(np.asarray(lam0, dtype=np.float32)
                                     .reshape(1, H)),
        "tri_in": tri,
        "id_in": ident,
        "id32_in": np.eye(128, dtype=np.float32),
    }
    nc = build_program(cfg)
    in_maps = [dict(base, cid=np.array([[c]], dtype=np.int32))
               for c in range(N_CORES)]
    res = run_bass_kernel_spmd(nc, in_maps, core_ids=list(range(N_CORES)))
    out = np.empty((T, D), dtype=np.float32)
    for c in range(N_CORES):
        lo, hi = cfg["bounds"][c], cfg["bounds"][c + 1]
        out[lo:hi] = np.asarray(res.results[c]["y"])[lo:hi]
    return out


def kernel(L0_attn_x, L0_attn_Wq, L0_attn_Wk, L0_attn_Wv, L0_attn_Wo,
           L0_attn_lambda_init, L0_ffn_weight, L0_ffn_rram_variation):
    cfg = FULL_CFG
    T, D = cfg["T"], cfg["D"]
    x = np.asarray(L0_attn_x, dtype=np.float32).reshape(T, D)
    out = run_cfg(cfg, x, L0_attn_Wq, L0_attn_Wk, L0_attn_Wv, L0_attn_Wo,
                  L0_attn_lambda_init, L0_ffn_weight,
                  L0_ffn_rram_variation)
    return out.reshape(1, T, D)


# version 12
# speedup vs baseline: 1.0269x; 1.0269x over previous
from contextlib import ExitStack

import numpy as np
import ml_dtypes

import concourse.bass as bass
import concourse.mybir as mybir
from concourse.bass_utils import run_bass_kernel_spmd

F32 = mybir.dt.float32
BF16 = mybir.dt.bfloat16
I32 = mybir.dt.int32
AF = mybir.ActivationFunctionType
ALU = mybir.AluOpType

N_CORES = 8

FULL_CFG = dict(
    T=8192,
    D=2048,
    H=16,
    bounds=[0, 2560, 4096, 5120, 6016, 6656, 7168, 7680, 8192],
)

SMALL_CFG = dict(
    T=2048,
    D=512,
    H=4,
    bounds=[0, 512, 896, 1152, 1408, 1664, 1792, 1920, 2048],
)

ENGS = ("sync", "tensor", "vector", "scalar", "gpsimd")


class Prog:

    def __init__(self, sems):
        self.q = {e: [] for e in ENGS}
        self.c = {name: 0 for name in sems}
        self.sems = sems

    def on(self, eng, fn):
        self.q[eng].append(fn)

    def inc(self, eng, op_fn, sem, n=1):
        h = self.sems[sem]
        self.on(eng, lambda e, f=op_fn, h=h, n=n: f(e).then_inc(h, n))
        self.c[sem] += n
        return self.c[sem]

    def plain(self, eng, op_fn):
        self.on(eng, lambda e, f=op_fn: f(e))

    def vec(self, op_fn, sem="sDve", n=1):
        assert sem == "sDve"
        self.wait("vector", "sDve")
        return self.inc("vector", op_fn, "sDve", n)

    def wait(self, eng, sem, val=None):
        h = self.sems[sem]
        v = self.c[sem] if val is None else val
        assert v <= self.c[sem], (sem, v, self.c[sem])
        self.on(eng, lambda e, h=h, v=v: e.wait_ge(h, v))
        return v


def emit_core(p, c, cfg, t):
    T, D, H = cfg["T"], cfg["D"], cfg["H"]
    DC = D // 128
    a, b = cfg["bounds"][c], cfg["bounds"][c + 1]
    own = b - a
    P = b
    maxown = max(cfg["bounds"][i + 1] - cfg["bounds"][i]
                 for i in range(N_CORES))

    wsb = t["w_sb"]
    NOC = D // 512 if D >= 512 else 1
    A_KH0, A_KH1, A_VH, A_VT = 0, T, 2 * T, 3 * T

    def groups(lo, hi):
        out = []
        g = lo
        while g < hi:
            out.append((g, min(512, hi - g)))
            g += 512
        return out

    def fm_view(dram_ap, col0, w, ncols):
        return dram_ap.rearrange("(dc p) t -> p dc t", p=128)[:, :, col0:col0 + w]

    def stage_kv_view(w):
        return t["stage"][:].rearrange("p (dc t) -> p dc t", t=512)[:, :, 0:w]

    def stage_ot_view(half):
        v = t["stage"][:, half * DC * 128:(half + 1) * DC * 128]
        return v.rearrange("p (dc t) -> p dc t", t=128)

    tct = {"n": 0, "evacs": [("sDve", 0)] * 4}

    def emit_transpose(src_bf16_ap_fn, dst_bf16_ap_fn, eng="vector"):
        k = tct["n"]
        slot = k % 4
        sem = "sDve" if eng == "vector" else "sAct"
        if k >= 2:
            pe, pv = tct["evacs"][(k - 2) % 4]
            p.wait("tensor", pe, pv)
        p.inc("tensor", lambda e, slot=slot, f=src_bf16_ap_fn: e.matmul(
            t["ps_s"][:, slot * 512:slot * 512 + 128],
            f(), t["id_sb"][:], start=True, stop=True), "sTen")
        if eng == "vector":
            p.wait("vector", "sTen")
            v = p.vec(lambda e, slot=slot, g=dst_bf16_ap_fn: e.tensor_copy(
                g(), t["ps_s"][:, slot * 512:slot * 512 + 128]))
        else:
            p.wait("scalar", "sTen")
            v = p.inc("scalar", lambda e, slot=slot, g=dst_bf16_ap_fn:
                      e.copy(g(), t["ps_s"][:, slot * 512:slot * 512 + 128]),
                      "sAct")
        tct["evacs"][slot] = (sem, v)
        tct["n"] = k + 1

    def emit_transpose_pair(src0_fn, src1_fn, dst_pair_fn, eng="vector"):
        k = tct["n"]
        assert k % 2 == 0
        s0, s1 = k % 4, (k + 1) % 4
        sem = "sDve" if eng == "vector" else "sAct"
        waits = {}
        for s in (s0, s1):
            pe, pv = tct["evacs"][s]
            waits[pe] = max(waits.get(pe, 0), pv)
        for pe, pv in waits.items():
            p.wait("tensor", pe, pv)
        p.plain("tensor", lambda e, s0=s0, f=src0_fn: e.matmul(
            t["ps_s"][:, s0 * 512:s0 * 512 + 128],
            f(), t["id_sb"][:], start=True, stop=True))
        v = p.inc("tensor", lambda e, s1=s1, f=src1_fn: e.matmul(
            t["ps_s"][:, s1 * 512:s1 * 512 + 128],
            f(), t["id_sb"][:], start=True, stop=True), "sTen")
        psrc = lambda s0=s0: t["ps_s"][:].rearrange(
            "p (b c) -> p b c", c=512)[:, s0:s0 + 2, 0:128]
        if eng == "vector":
            p.wait("vector", "sTen", v)
            ev = p.vec(lambda e, g=dst_pair_fn, psrc=psrc: e.tensor_copy(
                g(), psrc()))
        else:
            p.wait("scalar", "sTen", v)
            ev = p.inc("scalar", lambda e, g=dst_pair_fn, psrc=psrc:
                       e.copy(g(), psrc()), "sAct")
        tct["evacs"][s0] = tct["evacs"][s1] = (sem, ev)
        tct["n"] = k + 2

    WIDX = {"wk": 0, "wv": 1, "wq": 2, "wo": 3}

    WORD = ("wk", "wv", "wq", "wo")
    SPSEM = {"wv": "sSpW", "wq": "sSpWq", "wo": "sSpWo"}
    bg_q = []
    chunk_no = [0]
    espill_no = [0]

    def mk_chunk(wname, dc):
        base = WIDX[wname] * D

        def emit():
            ping = chunk_no[0] % 2
            chunk_no[0] += 1
            p.wait("scalar", "sDve")
            wv_ = p.inc("scalar", lambda e, dc=dc, w=t[wname], ping=ping:
                        e.dma_start(
                            out=t["wstage"][:, ping * D:(ping + 1) * D],
                            in_=w[dc * 128:(dc + 1) * 128, :]), "sW", 16)
            p.wait("vector", "sW", wv_)
            if wname == "wk":
                p.vec(lambda e, dc=dc, ping=ping: e.tensor_copy(
                    wsb[:, dc * D:(dc + 1) * D],
                    t["wstage"][:, ping * D:(ping + 1) * D]))
                return
            eping = espill_no[0] % 2
            espill_no[0] += 1
            p.wait("vector", "sSpW")
            p.wait("vector", "sSpWq")
            p.wait("vector", "sSpWo")
            cv = p.vec(lambda e, dc=dc, eping=eping, ping=ping: e.tensor_copy(
                t["e_sb"][:, eping * 2048:eping * 2048 + D],
                t["wstage"][:, ping * D:(ping + 1) * D]))
            p.wait("gpsimd", "sDve", cv)
            p.inc("gpsimd", lambda e, dc=dc, base=base, eping=eping:
                  e.dma_start(
                      out=t["wbf_d"].ap()[base + dc * 128:
                                          base + (dc + 1) * 128, :],
                      in_=t["e_sb"][:, eping * 2048:eping * 2048 + D]),
                  SPSEM[wname], 16)
        return emit

    for wname in WORD:
        for dc in range(DC):
            bg_q.append((wname, mk_chunk(wname, dc)))

    def bg_emit(n):
        for _ in range(min(n, len(bg_q))):
            bg_q.pop(0)[1]()

    def bg_flush(wname):
        oi = WORD.index(wname)
        while bg_q and WORD.index(bg_q[0][0]) <= oi:
            bg_q.pop(0)[1]()

    def load_weight_bf16(wname):
        bg_flush(wname)
        base = WIDX[wname] * D
        p.wait("sync", SPSEM[wname])
        p.wait("sync", "sAct")
        p.wait("sync", "sTen")
        p.inc("sync", lambda e, base=base: e.dma_start(
            out=wsb[:, 0:DC * D].rearrange("p (dc n) -> p dc n", n=D),
            in_=t["wbf_d"].ap()[base:base + D, :].rearrange(
                "(dc p) n -> p dc n", p=128)), "sW", 16)

    def emit_ln_group(g0, gw, hb, bgn=2):
        mmv, spv = xnt_state[hb]
        p.wait("vector", "sTen", mmv)
        p.wait("vector", "sSpXnT" if hb == 0 else "sSpXnT1", spv)
        for j in range(gw // 128):
            tok0 = g0 + j * 128
            xoff = (j % 2) * D
            p.wait("sync", "sDve")
            xv = p.inc("sync", lambda e, tok0=tok0, xoff=xoff: e.dma_start(
                out=t["xg"][:, xoff:xoff + D],
                in_=t["x"][tok0:tok0 + 128, :]), "sX", 16)
            p.wait("vector", "sX", xv)
            p.vec(lambda e, xoff=xoff: e.reduce_sum(
                t["nm"][:, 0:1], t["xg"][:, xoff:xoff + D],
                axis=mybir.AxisListType.X, negate=True))
            p.vec(lambda e: e.tensor_scalar_mul(
                t["nm"][:, 0:1], t["nm"][:, 0:1], 1.0 / D))
            p.vec(lambda e, xoff=xoff: e.scalar_tensor_tensor(
                t["sq"][:], t["xg"][:, xoff:xoff + D], 1.0,
                t["xg"][:, xoff:xoff + D], ALU.mult, ALU.mult,
                accum_out=t["ssq"][:, 0:1]))
            p.vec(lambda e: e.tensor_scalar_mul(
                t["var"][:, 0:1], t["ssq"][:, 0:1], 1.0 / D))
            p.vec(lambda e: e.tensor_mul(
                t["m2s"][:, 0:1], t["nm"][:, 0:1], t["nm"][:, 0:1]))
            p.vec(lambda e: e.tensor_sub(
                t["var"][:, 0:1], t["var"][:, 0:1], t["m2s"][:, 0:1]))
            dv = p.vec(lambda e: e.tensor_scalar_add(
                t["var"][:, 0:1], t["var"][:, 0:1], 1e-5), "sDve")
            p.wait("scalar", "sDve", dv)
            sv = p.inc("scalar", lambda e: e.activation(
                t["std"][:, 0:1], t["var"][:, 0:1], AF.Sqrt), "sAct")
            p.wait("vector", "sAct", sv)
            p.vec(lambda e: e.reciprocal(
                t["rstd"][:, 0:1], t["std"][:, 0:1]))
            p.wait("vector", "sTen")
            nv = p.vec(lambda e, xoff=xoff: e.tensor_scalar(
                t["xn"][:], t["xg"][:, xoff:xoff + D], t["nm"][:, 0:1],
                t["rstd"][:, 0:1], ALU.add, ALU.mult), "sDve")
            p.wait("tensor", "sDve", nv)
            for pr in range(DC // 2):
                emit_transpose_pair(
                    lambda pr=pr: t["xn"][:, (2 * pr) * 128:
                                          (2 * pr + 1) * 128],
                    lambda pr=pr: t["xn"][:, (2 * pr + 1) * 128:
                                          (2 * pr + 2) * 128],
                    lambda pr=pr, j=j, hb=hb: t["xnT_g"][
                        :, hb * DC * 512:(hb + 1) * DC * 512].rearrange(
                        "p (dcb c) -> p dcb c", c=512)[
                        :, 2 * pr:2 * pr + 2, j * 128:(j + 1) * 128],
                    eng="vector" if pr % 2 == 0 else "scalar")
            bg_emit(bgn)
        p.wait("gpsimd", "sDve")
        p.wait("gpsimd", "sAct")
        p.inc("gpsimd", lambda e, g0=g0, gw=gw, hb=hb: e.dma_start(
            out=fm_view(t["xnT_d"].ap(), g0, gw, T), in_=stage_xnT(gw, hb)),
            "sSpXnT" if hb == 0 else "sSpXnT1", 16)

    def stage_xnT(gw, hb):
        v = t["xnT_g"][:, hb * DC * 512:(hb + 1) * DC * 512]
        return v.rearrange("p (dc t) -> p dc t", t=512)[:, :, 0:gw]

    def proj_pass(wname, dst_dram, glist, colshift, spsem):
        if wname == "wk":
            bg_emit(4)
        else:
            load_weight_bf16(wname)
            p.wait("sync", "sSpXnT")
            p.wait("sync", "sSpXnT1")
        for gi, (g0, gw) in enumerate(glist):
            hb = gi % 2
            base = hb * DC * 512
            if wname == "wk":
                emit_ln_group(g0, gw, hb, bgn=3 if gi == 0 else 2)
                if gi == 0:
                    bg_flush("wk")
            else:
                mmv, spv = xnt_state[hb]
                p.wait("sync", "sTen", mmv)
                rv = p.inc("sync", lambda e, g0=g0, gw=gw, hb=hb: e.dma_start(
                    out=stage_xnT(gw, hb),
                    in_=fm_view(t["xnT_d"].ap(), g0, gw, T)),
                    "sXnTr" if hb == 0 else "sXnTr1", 16)
            p.wait("scalar", "sSpK")
            p.wait("scalar", "sSpV")
            p.wait("scalar", "sSpQ")
            ready_done = [False]
            for m in range(DC):
                acc = t["ps_cd"][:, (m % 2) * 512:(m % 2) * 512 + gw]
                if not ready_done[0]:
                    p.wait("tensor", "sW")
                    if wname == "wk":
                        p.wait("tensor", "sDve")
                    p.wait("tensor", "sAct")
                    if wname != "wk":
                        p.wait("tensor", "sXnTr" if hb == 0 else "sXnTr1")
                    ready_done[0] = True
                if m >= 2:
                    p.wait("tensor", "sAct", evac_val[m % 2])
                for dc in range(DC):
                    is_last = dc == DC - 1
                    fn = lambda e, acc=acc, m=m, dc=dc, gw=gw, base=base: \
                        e.matmul(
                            acc,
                            wsb[:, dc * D + m * 128:dc * D + (m + 1) * 128],
                            t["xnT_g"][:, base + dc * 512:
                                       base + dc * 512 + gw],
                            start=(dc == 0), stop=(dc == DC - 1))
                    if is_last:
                        vt = p.inc("tensor", fn, "sTen")
                    else:
                        p.plain("tensor", fn)
                p.wait("scalar", "sTen", vt)
                evac_val[m % 2] = p.inc(
                    "scalar", lambda e, m=m, gw=gw, acc=acc: e.copy(
                        t["stage"][:, m * 512:m * 512 + gw], acc), "sAct")
            xnt_state[hb] = (p.c["sTen"], p.c["sSpXnT" if hb == 0 else "sSpXnT1"])
            p.wait("gpsimd", "sAct")
            p.inc("gpsimd", lambda e, g0=g0, gw=gw, d=dst_dram, cs=colshift,
                  nc_=(T if dst_dram is not t["q_d"] else maxown):
                  e.dma_start(out=fm_view(d.ap(), g0 - cs, gw, nc_),
                              in_=stage_kv_view(gw)), spsem, 16)
            bg_emit(4)

    evac_val = {0: 0, 1: 0}
    xnt_state = {0: (0, 0), 1: (0, 0)}
    stop_after = cfg.get("stop_after")

    def dbg_dump(src_dram, rows, cols):
        p.wait("sync", "sSpK")
        p.wait("sync", "sSpV")
        p.wait("sync", "sSpQ")
        p.wait("sync", "sSpOT")
        p.inc("sync", lambda e: e.dma_start(
            out=t["qh"][:, 0:cols], in_=src_dram.ap()[0:128, 0:cols]),
            "sKH", 16)
        p.wait("vector", "sKH")
        dv = p.vec(lambda e: e.tensor_copy(
            t["sq"][:, 0:cols], t["qh"][:, 0:cols]))
        p.wait("gpsimd", "sDve", dv)
        p.inc("gpsimd", lambda e: e.dma_start(
            out=t["y"][0:128, 0:cols], in_=t["sq"][:, 0:cols]), "sSpY", 16)
        p.wait("gpsimd", "sSpY")

    if stop_after == "P0":
        return
    pre_groups = groups(0, P)
    own_groups = groups(a, b)
    if stop_after == "W":
        bg_flush("wk")
        p.wait("gpsimd", "sDve")
        return
    if stop_after == "G1":
        bg_flush("wk")
        emit_ln_group(pre_groups[0][0], pre_groups[0][1], 0)
        p.wait("gpsimd", "sSpXnT")
        return
    proj_pass("wk", t["k_d"], pre_groups[:1] if stop_after == "G1K"
              else pre_groups, 0, "sSpK")
    if stop_after == "G1K":
        p.wait("gpsimd", "sSpK")
        return
    if stop_after == "P1a":
        dbg_dump(t["k_d"], 128, min(512, D))
        return
    proj_pass("wv", t["v_d"], pre_groups, 0, "sSpV")
    if stop_after == "P1b":
        p.wait("gpsimd", "sSpV")
        return
    proj_pass("wq", t["q_d"], own_groups, a, "sSpQ")
    if stop_after == "P1":
        dbg_dump(t["k_d"], 128, min(512, D))
        return

    qtiles = groups(a, b)
    NKC = P // 128

    p.wait("sync", "sSpK")
    p.wait("sync", "sSpV")
    p.wait("sync", "sSpQ")
    p.wait("sync", "sSpWq")
    p.wait("sync", "sSpWo")
    p.wait("vector", "sAct")

    def emit_av(q0, qw, nkc, kc, eb, exp_v, mask_v, final=False):
        if kc == 0:
            p.wait("tensor", "sDve")
        p.wait("tensor", "sAct", exp_v)
        if mask_v is not None:
            p.wait("tensor", "sDve", mask_v)
        for half in range(2):
            col = eb * 2048 + half * 512
            p.plain("tensor", lambda e, kc=kc, half=half, col=col, qw=qw,
                    nkc=nkc: e.matmul(
                        t["ps_cd"][:, half * 512:half * 512 + qw],
                        wsb[:, A_VT + kc * 128:A_VT + (kc + 1) * 128],
                        t["e_sb"][:, col:col + qw],
                        start=(kc == 0), stop=(kc == nkc - 1)))
        ret = None
        for half in range(2):
            col = eb * 2048 + half * 512
            row = 32 * half
            fn = lambda e, col=col, row=row, qw=qw, kc=kc, nkc=nkc: e.matmul(
                t["ps_e"][row:row + 1, 0:qw],
                t["ones_col"][:, 0:1],
                t["e_sb"][:, col:col + qw],
                start=(kc == 0), stop=(kc == nkc - 1),
                tile_position=(0, row))
            if final and half == 1:
                ret = p.inc("tensor", fn, "sTen")
            else:
                p.plain("tensor", fn)
        return ret

    khv, qhv, vhv = {}, {}, {}

    def load_kq(h):
        kh0 = A_KH0 if h % 2 == 0 else A_KH1
        ks = "sKH" if h % 2 == 0 else "sKH1"
        qs = "sQH" if h % 2 == 0 else "sQH1"
        khv[h] = (ks, p.inc("sync", lambda e, h=h, kh0=kh0: e.dma_start(
            out=wsb[:, kh0:kh0 + P],
            in_=t["k_d"].ap()[h * 128:(h + 1) * 128, 0:P]), ks, 16))
        qoff0 = (h % 2) * maxown
        qhv[h] = (qs, p.inc("sync", lambda e, h=h, qoff0=qoff0: e.dma_start(
            out=t["qh"][:, qoff0:qoff0 + own],
            in_=t["q_d"].ap()[h * 128:(h + 1) * 128, 0:own]), qs, 16))

    def load_v(h):
        vhv[h] = p.inc("sync", lambda e, h=h: e.dma_start(
            out=wsb[:, A_VH:A_VH + P],
            in_=t["v_d"].ap()[h * 128:(h + 1) * 128, 0:P]), "sVH", 16)

    load_kq(0)
    load_v(0)
    ping_last = {0: 0, 1: 0}

    for h in range(H):
        kh0 = A_KH0 if h % 2 == 0 else A_KH1
        qoff0 = (h % 2) * maxown
        p.wait("tensor", "sDve")
        p.wait("tensor", "sVH", vhv[h])
        for kc in range(NKC):
            emit_transpose(
                lambda kc=kc: wsb[:, A_VH + kc * 128:A_VH + (kc + 1) * 128],
                lambda kc=kc: wsb[:, A_VT + kc * 128:A_VT + (kc + 1) * 128],
                eng="scalar")
        vt_done = p.c["sTen"]
        if h + 1 < H:
            p.wait("sync", "sTen", vt_done)
            load_kq(h + 1)
            load_v(h + 1)
        p.wait("tensor", khv[h][0], khv[h][1])
        p.wait("tensor", qhv[h][0], qhv[h][1])
        p.wait("tensor", "sDve")
        p.wait("tensor", "sAct")

        for (q0, qw) in qtiles:
            qoff = qoff0 + (q0 - a)
            nkc = (q0 + qw) // 128
            from collections import deque
            pend = deque()

            for bi in range(nkc):
                kc = bi
                ping = bi % 2
                eb = bi % 3
                base = ping * 1024
                p.wait("tensor", "sAct", ping_last[ping])
                sc_v = None
                for half in range(2):
                    col = base + half * 512
                    fn = lambda e, kc=kc, half=half, col=col, qw=qw, \
                        qoff=qoff, kh0=kh0: e.matmul(
                            t["ps_s"][:, col:col + qw],
                            wsb[half * 64:(half + 1) * 64,
                                kh0 + kc * 128:kh0 + (kc + 1) * 128],
                            t["qh"][half * 64:(half + 1) * 64,
                                    qoff:qoff + qw],
                            start=True, stop=True,
                            tile_position=(half * 64, 0))
                    if half == 1:
                        sc_v = p.inc("tensor", fn, "sTen")
                    else:
                        p.plain("tensor", fn)
                if len(pend) >= 2:
                    emit_av(q0, qw, nkc, *pend.popleft())
                p.wait("scalar", "sTen", sc_v)
                width = 512 + qw
                ping_last[ping] = p.inc(
                    "scalar", lambda e, eb=eb, base=base, width=width:
                    e.activation(
                        t["e_sb"][:, eb * 2048:eb * 2048 + width],
                        t["ps_s"][:, base:base + width],
                        AF.Exp, scale=t["const"][:, 0:1]), "sAct")
                mask_v = None
                if kc * 128 >= q0:
                    cstart = kc * 128 - q0
                    p.wait("vector", "sAct", ping_last[ping])
                    for half in range(2):
                        col = eb * 2048 + half * 512
                        if cstart > 0:
                            p.vec(lambda e, col=col, cs=cstart: e.memset(
                                t["e_sb"][:, col:col + cs], 0.0))
                        fn = lambda e, col=col, cs=cstart: e.tensor_mul(
                            t["e_sb"][:, col + cs:col + cs + 128],
                            t["e_sb"][:, col + cs:col + cs + 128],
                            t["tri"][:])
                        mask_v = p.vec(fn)
                pend.append((kc, eb, ping_last[ping], mask_v))

            while len(pend) > 1:
                emit_av(q0, qw, nkc, *pend.popleft())
            last_mm = emit_av(q0, qw, nkc, *pend.popleft(), final=True)

            p.wait("vector", "sTen", last_mm)
            p.wait("vector", "sSpOT")
            p.vec(lambda e, qw=qw: e.tensor_copy(
                t["sra"][:, 0:qw], t["ps_e"][0:64, 0:qw]))
            def _recip(e, qw=qw):
                with t["nc"].allow_low_precision(
                        reason="softmax denominators broadcast in bf16"):
                    return e.reciprocal(t["rr"][0:33, 0:qw],
                                        t["sra"][0:33, 0:qw])
            rec_v = p.vec(_recip)
            p.wait("tensor", "sDve", rec_v)
            p.plain("tensor", lambda e, qw=qw: e.matmul(
                t["ps_sp"][:, 0:qw], t["ones_row"][0:1, :],
                t["rr"][0:1, 0:qw], start=True, stop=True))
            bc_v = p.inc("tensor", lambda e, qw=qw: e.matmul(
                t["ps_e"][:, 0:qw], t["mones_row"][32:33, :],
                t["rr"][32:33, 0:qw], start=True, stop=True), "sTen")
            p.wait("vector", "sTen", bc_v)
            p.vec(lambda e, qw=qw: e.tensor_copy(
                t["fb"][:, 0:qw], t["ps_sp"][:, 0:qw]))
            p.vec(lambda e, qw=qw: e.tensor_copy(
                t["fb"][:, 512:512 + qw], t["ps_e"][:, 0:qw]))
            p.vec(lambda e, qw=qw: e.tensor_mul(
                t["m1"][:, 0:qw], t["ps_cd"][:, 0:qw], t["fb"][:, 0:qw]))
            p.vec(lambda e, qw=qw: e.tensor_mul(
                t["m2"][:, 0:qw], t["ps_cd"][:, 512:512 + qw],
                t["fb"][:, 512:512 + qw]))
            p.vec(lambda e, qw=qw, h=h: e.tensor_scalar(
                t["m2"][:, 0:qw], t["m2"][:, 0:qw],
                t["lam_bc"][:, h:h + 1], None, ALU.mult))
            cmb_v = p.vec(lambda e, qw=qw: e.tensor_add(
                t["oT"][:, 0:qw], t["m1"][:, 0:qw], t["m2"][:, 0:qw]))
            p.wait("gpsimd", "sDve", cmb_v)
            p.inc("gpsimd", lambda e, h=h, qoff=qoff, q0=q0, qw=qw:
                  e.dma_start(
                      out=t["oT_d"].ap()[h * 128:(h + 1) * 128,
                                         q0 - a:q0 - a + qw],
                      in_=t["oT"][:, 0:qw]), "sSpOT", 16)

    if stop_after == "P2":
        dbg_dump(t["oT_d"], 128, min(512, own))
        return
    NTT = own // 128
    p.wait("sync", "sSpOT")
    p.wait("vector", "sTen")
    load_weight_bf16("wo")

    for i in range(NTT):
        half = i % 2
        p.wait("sync", "sTen")
        otr = p.inc("sync", lambda e, i=i, half=half: e.dma_start(
            out=stage_ot_view(half),
            in_=fm_view(t["oT_d"].ap(), i * 128, 128, maxown)), "sOTr", 16)
        p.wait("sync", "sDve")
        xv = p.inc("sync", lambda e, i=i, half=half: e.dma_start(
            out=t["xg"][:, half * D:half * D + D],
            in_=t["x"][a + i * 128:a + i * 128 + 128, :]), "sX", 16)
        p.wait("tensor", "sOTr", otr)
        p.wait("tensor", "sW")
        p.wait("tensor", "sDve")
        wo_v = None
        for oc in range(NOC):
            for hc in range(DC):
                is_last = (oc == NOC - 1) and (hc == DC - 1)
                fn = lambda e, oc=oc, hc=hc, half=half: e.matmul(
                    t["ps_s"][:, oc * 512:(oc + 1) * 512],
                    t["stage"][:, half * DC * 128 + hc * 128:
                               half * DC * 128 + (hc + 1) * 128],
                    wsb[:, hc * D + oc * 512:hc * D + (oc + 1) * 512],
                    start=(hc == 0), stop=(hc == DC - 1))
                if is_last:
                    wo_v = p.inc("tensor", fn, "sTen")
                else:
                    p.plain("tensor", fn)
        p.wait("vector", "sTen", wo_v)
        p.wait("vector", "sX", xv)
        p.wait("vector", "sSpX2")
        for oc in range(NOC):
            p.vec(lambda e, oc=oc, half=half: e.tensor_add(
                t["sq"][:, oc * 512:(oc + 1) * 512],
                t["ps_s"][:, oc * 512:(oc + 1) * 512],
                t["xg"][:, half * D + oc * 512:half * D + (oc + 1) * 512]))
        p.vec(lambda e: e.reduce_sum(
            t["nm"][:, 0:1], t["sq"][:], axis=mybir.AxisListType.X,
            negate=True))
        x2_v = p.vec(lambda e, i=i: e.tensor_scalar_mul(
            t["stats"][:, i:i + 1], t["nm"][:, 0:1], 1.0 / D),
            "sDve")
        p.wait("scalar", "sDve", x2_v)
        sq_v = p.inc("scalar", lambda e: e.activation(
            t["wstage"][:, 0:D], t["sq"][:], AF.Square,
            accum_out=t["ssq"][:, 0:1]), "sAct")
        p.wait("vector", "sAct", sq_v)
        p.vec(lambda e: e.tensor_scalar_mul(
            t["var"][:, 0:1], t["ssq"][:, 0:1], 1.0 / D))
        p.vec(lambda e, i=i: e.tensor_mul(
            t["m2s"][:, 0:1], t["stats"][:, i:i + 1],
            t["stats"][:, i:i + 1]))
        p.vec(lambda e: e.tensor_sub(
            t["var"][:, 0:1], t["var"][:, 0:1], t["m2s"][:, 0:1]))
        st_v = p.vec(lambda e, i=i, NTT=NTT: e.tensor_scalar_add(
            t["stats"][:, NTT + i:NTT + i + 1], t["var"][:, 0:1], 1e-5),
            "sDve")
        p.wait("gpsimd", "sDve", st_v)
        p.inc("gpsimd", lambda e, i=i: e.dma_start(
            out=t["x2_d"].ap()[i * 128:(i + 1) * 128, :], in_=t["sq"][:]),
            "sSpX2", 16)

    p.wait("scalar", "sDve")
    rs_v = p.inc("scalar", lambda e, NTT=NTT: e.activation(
        t["stats2"][:, 0:NTT], t["stats"][:, NTT:2 * NTT], AF.Sqrt),
        "sAct")
    p.wait("vector", "sAct", rs_v)
    p.vec(lambda e, NTT=NTT: e.reciprocal(
        t["stats3"][:, 0:NTT], t["stats2"][:, 0:NTT]), "sDve")

    p.wait("vector", "sSpX2")
    for och in range(DC):
        hw = och % 2
        if och < 2:
            p.wait("sync", "sAct")
            p.wait("sync", "sDve")
        else:
            p.wait("sync", "sAct")
            p.wait("sync", "sDve")
            p.wait("sync", "sW")
        p.inc("sync", lambda e, och=och, hw=hw: e.dma_start(
            out=t["wstage"][:, hw * D:(hw + 1) * D],
            in_=t["fw"][och * 128:(och + 1) * 128, :]), "sW", 16)
        wv_ = p.inc("sync", lambda e, och=och, hw=hw: e.dma_start(
            out=t["xg"][:, hw * D:(hw + 1) * D],
            in_=t["frr"][och * 128:(och + 1) * 128, :]), "sW", 16)
        p.wait("scalar", "sW", wv_)
        p.wait("scalar", "sDve")
        p.plain("scalar", lambda e, hw=hw: e.activation(
            t["e_sb"][:, 0:D], t["wstage"][:, hw * D:(hw + 1) * D], AF.Sign,
            bias=t["const"][:, 3:4]))
        sg_v = p.inc("scalar", lambda e, hw=hw: e.activation(
            t["e_sb"][:, 2048:2048 + D], t["wstage"][:, hw * D:(hw + 1) * D],
            AF.Sign, bias=t["const"][:, 4:5]), "sAct")
        p.wait("vector", "sAct", sg_v)
        p.vec(lambda e: e.tensor_add(
            t["xn"][:], t["e_sb"][:, 0:D], t["e_sb"][:, 2048:2048 + D]))
        p.wait("vector", "sTen")
        wr_v = p.vec(lambda e, hw=hw: e.scalar_tensor_tensor(
            t["e_sb"][:, 0:D], t["xn"][:], 0.3,
            t["xg"][:, hw * D:(hw + 1) * D], ALU.mult, ALU.add), "sDve")
        p.wait("tensor", "sDve", wr_v)
        for dc in range(DC):
            emit_transpose(
                lambda dc=dc: t["e_sb"][:, dc * 128:(dc + 1) * 128],
                lambda dc=dc, och=och: wsb[:, dc * D + och * 128:
                                           dc * D + och * 128 + 128],
                eng="vector" if dc % 2 == 0 else "scalar")

    p.wait("vector", "sSpX2")
    p.wait("sync", "sSpX2")
    for i in range(NTT):
        half = i % 2
        p.wait("sync", "sDve")
        x2r = p.inc("sync", lambda e, i=i, half=half: e.dma_start(
            out=t["xg"][:, half * D:half * D + D],
            in_=t["x2_d"].ap()[i * 128:(i + 1) * 128, :]), "sX2r", 16)
        p.wait("vector", "sX2r", x2r)
        p.wait("vector", "sAct")
        nv = p.vec(lambda e, i=i, half=half: e.tensor_scalar(
            t["xn"][:], t["xg"][:, half * D:half * D + D],
            t["stats"][:, i:i + 1],
            t["stats3"][:, i:i + 1],
            ALU.add, ALU.mult), "sDve")
        p.wait("scalar", "sDve", nv)
        p.plain("scalar", lambda e: e.activation(
            t["e_sb"][:, 0:D], t["xn"][:], AF.Sign, bias=t["const"][:, 1:2]))
        sg_v = p.inc("scalar", lambda e: e.activation(
            t["e_sb"][:, 2048:2048 + D], t["xn"][:], AF.Sign,
            bias=t["const"][:, 2:3]), "sAct")
        p.wait("vector", "sAct", sg_v)
        p.vec(lambda e: e.tensor_add(
            t["xn"][:], t["e_sb"][:, 0:D], t["e_sb"][:, 2048:2048 + D]))
        p.wait("vector", "sTen")
        xq_v = p.vec(lambda e: e.tensor_scalar(
            t["e_sb"][:, 0:D], t["xn"][:], 0.25, None, ALU.mult), "sDve")
        p.wait("tensor", "sDve", xq_v)
        for dc in range(DC):
            emit_transpose(
                lambda dc=dc: t["e_sb"][:, dc * 128:(dc + 1) * 128],
                lambda dc=dc, half=half: t["stage"][
                    :, half * DC * 128 + dc * 128:
                    half * DC * 128 + (dc + 1) * 128])
        xqT_v = p.wait("tensor", "sDve")
        ffn_v = None
        for oc in range(NOC):
            for dc in range(DC):
                is_last = (oc == NOC - 1) and (dc == DC - 1)
                fn = lambda e, oc=oc, dc=dc, half=half: e.matmul(
                    t["ps_s"][:, oc * 512:(oc + 1) * 512],
                    t["stage"][:, half * DC * 128 + dc * 128:
                               half * DC * 128 + (dc + 1) * 128],
                    wsb[:, dc * D + oc * 512:dc * D + (oc + 1) * 512],
                    start=(dc == 0), stop=(dc == DC - 1))
                if is_last:
                    ffn_v = p.inc("tensor", fn, "sTen")
                else:
                    p.plain("tensor", fn)
        p.wait("vector", "sTen", ffn_v)
        p.wait("vector", "sSpY")
        out_v = None
        for oc in range(NOC):
            is_last = oc == NOC - 1
            fn = lambda e, oc=oc, half=half: e.tensor_add(
                t["sq"][:, oc * 512:(oc + 1) * 512],
                t["ps_s"][:, oc * 512:(oc + 1) * 512],
                t["xg"][:, half * D + oc * 512:half * D + (oc + 1) * 512])
            if is_last:
                out_v = p.vec(fn, "sDve")
            else:
                p.vec(fn)
        p.wait("gpsimd", "sDve", out_v)
        p.inc("gpsimd", lambda e, i=i: e.dma_start(
            out=t["y"][a + i * 128:a + i * 128 + 128, :], in_=t["sq"][:]),
            "sSpY", 16)

    p.wait("gpsimd", "sSpY")


def build_program(cfg):
    T, D, H = cfg["T"], cfg["D"], cfg["H"]
    DC = D // 128
    maxown = max(cfg["bounds"][i + 1] - cfg["bounds"][i]
                 for i in range(N_CORES))
    w_cols = max(DC * D, 4 * T)

    nc = bass.Bass("TRN2", target_bir_lowering=False)
    t = {"nc": nc}
    t["x"] = nc.dram_tensor("x", [T, D], F32, kind="ExternalInput").ap()
    for n_, nm in [("wq", "wq"), ("wk", "wk"), ("wv", "wv"), ("wo", "wo"),
                   ("fw", "fw"), ("frr", "frr")]:
        t[n_] = nc.dram_tensor(nm, [D, D], F32, kind="ExternalInput").ap()
    t["lam0"] = nc.dram_tensor("lam0", [1, H], F32, kind="ExternalInput").ap()
    t["cid"] = nc.dram_tensor("cid", [1, 1], I32, kind="ExternalInput").ap()
    t["tri_in"] = nc.dram_tensor("tri_in", [128, 128], BF16,
                                 kind="ExternalInput").ap()
    t["id_in"] = nc.dram_tensor("id_in", [128, 128], BF16,
                                kind="ExternalInput").ap()
    t["id32_in"] = nc.dram_tensor("id32_in", [128, 128], F32,
                                  kind="ExternalInput").ap()
    t["y"] = nc.dram_tensor("y", [T, D], F32, kind="ExternalOutput").ap()

    t["xnT_d"] = nc.dram_tensor("xnT_d", [D, T], BF16)
    t["k_d"] = nc.dram_tensor("k_d", [D, T], BF16)
    t["v_d"] = nc.dram_tensor("v_d", [D, T], BF16)
    t["q_d"] = nc.dram_tensor("q_d", [D, maxown], BF16)
    t["oT_d"] = nc.dram_tensor("oT_d", [D, maxown], BF16)
    t["x2_d"] = nc.dram_tensor("x2_d", [maxown, D], F32)
    t["wbf_d"] = nc.dram_tensor("wbf_d", [4 * D, D], BF16)

    sem_names = ["sIn", "sX", "sW", "sKH", "sKH1", "sVH", "sQH", "sQH1", "sOTr", "sX2r",
                 "sXnTr", "sXnTr1", "sSpW", "sSpWq", "sSpWo", "sSpXnT", "sSpXnT1", "sSpK", "sSpV", "sSpQ", "sSpOT",
                 "sSpX2", "sSpY", "sTen", "sAct", "sDve"]

    with ExitStack() as ctx:
        sb = lambda n, s, d: ctx.enter_context(nc.sbuf_tensor(n, s, d))
        psa = lambda n, s, d: ctx.enter_context(nc.psum_tensor(n, s, d))

        t["cid_sb"] = sb("cid_sb", [1, 1], I32)
        t["const"] = sb("const", [128, 8], F32)
        t["tri"] = sb("tri", [128, 128], BF16)
        t["id_sb"] = sb("id_sb", [128, 128], BF16)
        t["id32"] = sb("id32", [128, 128], F32)
        t["ones_col"] = sb("ones_col", [128, 1], BF16)
        t["ones_row"] = sb("ones_row", [1, 128], BF16)
        t["ones_row32"] = sb("ones_row32", [1, 128], F32)
        t["mones_row"] = sb("mones_row", [33, 128], BF16)
        t["lamraw"] = sb("lamraw", [1, H], F32)
        t["lam"] = sb("lam", [1, H], F32)
        t["lamt"] = sb("lamt", [1, H], F32)
        t["lam_bc"] = sb("lam_bc", [128, H], F32)

        t["w_sb"] = sb("w_sb", [128, w_cols], BF16)
        t["wstage"] = sb("wstage", [128, 2 * D], F32)
        t["qh"] = sb("qh", [128, 2 * maxown], BF16)
        t["stage"] = sb("stage", [128, DC * 512], BF16)
        t["e_sb"] = sb("e_sb", [128, 6144], BF16)
        t["xg"] = sb("xg", [128, 2 * D], F32)
        t["sq"] = sb("sq", [128, D], F32)
        t["xn"] = sb("xn", [128, D], BF16)
        t["xnT_g"] = sb("xnT_g", [128, 2 * DC * 512], BF16)
        t["fb"] = sb("fb", [128, 1024], F32)
        t["m1"] = sb("m1", [128, 512], F32)
        t["m2"] = sb("m2", [128, 512], F32)
        t["oT"] = sb("oT", [128, 512], BF16)
        t["sra"] = sb("sra", [64, 512], F32)
        t["srb"] = sb("srb", [64, 512], F32)
        t["rr"] = sb("rr", [64, 512], BF16)
        t["nm"] = sb("nm", [128, 1], F32)
        t["ssq"] = sb("ssq", [128, 1], F32)
        t["var"] = sb("var", [128, 1], F32)
        t["std"] = sb("std", [128, 1], F32)
        t["rstd"] = sb("rstd", [128, 1], F32)
        t["m2s"] = sb("m2s", [128, 1], F32)
        t["stats"] = sb("stats", [128, 2 * (maxown // 128)], F32)
        t["stats2"] = sb("stats2", [128, 2 * (maxown // 128)], F32)
        t["stats3"] = sb("stats3", [128, 2 * (maxown // 128)], F32)

        t["ps_s"] = psa("ps_s", [128, 2048], F32)
        t["ps_cd"] = psa("ps_cd", [128, 1024], F32)
        t["ps_e"] = psa("ps_e", [128, 512], F32)
        t["ps_sp"] = psa("ps_sp", [128, 512], F32)

        sems = {n: ctx.enter_context(nc.semaphore(n)) for n in sem_names}
        regs = {e: ctx.enter_context(getattr(nc, e).register("r_" + e))
                for e in ENGS}
        block = ctx.enter_context(nc.Block())

        progs = []
        for c in range(N_CORES):
            p = Prog(sems)
            p.c["sIn"] = 80
            p.c["sDve"] = 15
            p.c["sTen"] = 1
            p.c["sAct"] = 1
            emit_core(p, c, cfg, t)
            progs.append(p)

        def replay(eng_name, e):
            e.wait_ge(sems["sIn"], 80)
            e.reg_load(regs[eng_name], t["cid_sb"][0:1, 0:1])
            if eng_name == "vector":
                e.memset(t["const"][:, 0:1], 0.125).then_inc(sems["sDve"], 1)
                e.memset(t["const"][:, 1:2], -0.33).then_inc(sems["sDve"], 1)
                e.memset(t["const"][:, 2:3], 0.33).then_inc(sems["sDve"], 1)
                e.memset(t["const"][:, 3:4], -0.2).then_inc(sems["sDve"], 1)
                e.memset(t["const"][:, 4:5], 0.2).then_inc(sems["sDve"], 1)
                e.memset(t["ones_col"][:], 1.0).then_inc(sems["sDve"], 1)
                e.memset(t["ones_row"][:], 1.0).then_inc(sems["sDve"], 1)
                e.memset(t["mones_row"][:], -1.0).then_inc(sems["sDve"], 1)
                e.memset(t["ones_row32"][:], 1.0).then_inc(sems["sDve"], 1)
                e.memset(t["ps_s"][:], 0.0).then_inc(sems["sDve"], 1)
                e.memset(t["ps_cd"][:], 0.0).then_inc(sems["sDve"], 1)
                e.memset(t["ps_e"][:], 0.0).then_inc(sems["sDve"], 1)
            if eng_name == "scalar":
                e.wait_ge(sems["sDve"], 1)
                e.activation(t["lamt"][0:1, :], t["lamraw"][0:1, :], AF.Exp,
                             scale=-1.0).then_inc(sems["sAct"], 1)
            if eng_name == "vector":
                e.wait_ge(sems["sAct"], 1)
                e.wait_ge(sems["sDve"], 12)
                e.tensor_scalar_add(t["lamt"][0:1, :], t["lamt"][0:1, :],
                                    1.0).then_inc(sems["sDve"], 1)
                e.wait_ge(sems["sDve"], 13)
                e.reciprocal(t["lam"][0:1, :], t["lamt"][0:1, :]).then_inc(
                    sems["sDve"], 1)
            if eng_name == "tensor":
                e.wait_ge(sems["sDve"], 14)
                e.matmul(t["ps_e"][:, 0:H], t["ones_row32"][0:1, :],
                         t["lam"][0:1, :], start=True, stop=True).then_inc(
                    sems["sTen"], 1)
            if eng_name == "vector":
                e.wait_ge(sems["sTen"], 1)
                e.wait_ge(sems["sDve"], 14)
                e.tensor_copy(t["lam_bc"][:, 0:H],
                              t["ps_e"][:, 0:H]).then_inc(sems["sDve"], 1)
            IF_CHUNK = 1 << 30
            for c in range(N_CORES):
                q = progs[c].q[eng_name]
                for i0 in range(0, len(q), IF_CHUNK):
                    with e.If_eq(regs[eng_name], c):
                        for fn in q[i0:i0 + IF_CHUNK]:
                            fn(e)

        @block.sync
        def _(sync):
            sync.dma_start(out=t["cid_sb"][:], in_=t["cid"]).then_inc(
                sems["sIn"], 16)
            sync.dma_start(out=t["tri"][:], in_=t["tri_in"]).then_inc(
                sems["sIn"], 16)
            sync.dma_start(out=t["id_sb"][:], in_=t["id_in"]).then_inc(
                sems["sIn"], 16)
            sync.dma_start(out=t["id32"][:], in_=t["id32_in"]).then_inc(
                sems["sIn"], 16)
            sync.dma_start(out=t["lamraw"][0:1, :], in_=t["lam0"]).then_inc(
                sems["sIn"], 16)
            replay("sync", sync)

        @block.tensor
        def _(e):
            replay("tensor", e)

        @block.vector
        def _(e):
            replay("vector", e)

        @block.scalar
        def _(e):
            replay("scalar", e)

        @block.gpsimd
        def _(e):
            replay("gpsimd", e)

    return nc


def make_consts():
    tri = (np.arange(128)[:, None] <= np.arange(128)[None, :]).astype(
        ml_dtypes.bfloat16)
    ident = np.eye(128, dtype=ml_dtypes.bfloat16)
    return tri, ident


def run_cfg(cfg, x, wq, wk, wv, wo, lam0, fw, frr):
    T, D, H = cfg["T"], cfg["D"], cfg["H"]
    tri, ident = make_consts()
    base = {
        "x": np.ascontiguousarray(x, dtype=np.float32),
        "wq": np.ascontiguousarray(wq, dtype=np.float32),
        "wk": np.ascontiguousarray(wk, dtype=np.float32),
        "wv": np.ascontiguousarray(wv, dtype=np.float32),
        "wo": np.ascontiguousarray(wo, dtype=np.float32),
        "fw": np.ascontiguousarray(fw, dtype=np.float32),
        "frr": np.ascontiguousarray(frr, dtype=np.float32),
        "lam0": np.ascontiguousarray(np.asarray(lam0, dtype=np.float32)
                                     .reshape(1, H)),
        "tri_in": tri,
        "id_in": ident,
        "id32_in": np.eye(128, dtype=np.float32),
    }
    nc = build_program(cfg)
    in_maps = [dict(base, cid=np.array([[c]], dtype=np.int32))
               for c in range(N_CORES)]
    res = run_bass_kernel_spmd(nc, in_maps, core_ids=list(range(N_CORES)))
    out = np.empty((T, D), dtype=np.float32)
    for c in range(N_CORES):
        lo, hi = cfg["bounds"][c], cfg["bounds"][c + 1]
        out[lo:hi] = np.asarray(res.results[c]["y"])[lo:hi]
    return out


def kernel(L0_attn_x, L0_attn_Wq, L0_attn_Wk, L0_attn_Wv, L0_attn_Wo,
           L0_attn_lambda_init, L0_ffn_weight, L0_ffn_rram_variation):
    cfg = FULL_CFG
    T, D = cfg["T"], cfg["D"]
    x = np.asarray(L0_attn_x, dtype=np.float32).reshape(T, D)
    out = run_cfg(cfg, x, L0_attn_Wq, L0_attn_Wk, L0_attn_Wv, L0_attn_Wo,
                  L0_attn_lambda_init, L0_ffn_weight,
                  L0_ffn_rram_variation)
    return out.reshape(1, T, D)
